# version 03
# baseline (speedup 1.0000x reference)
import numpy as np
import ml_dtypes
from contextlib import ExitStack

import concourse.bass as bass
import concourse.mybir as mybir
from concourse.bass_utils import run_bass_kernel_spmd

F32 = mybir.dt.float32
BF16 = mybir.dt.bfloat16
AF = mybir.ActivationFunctionType
ALU = mybir.AluOpType
BF = ml_dtypes.bfloat16

H = 1024
NT = 8
TOK = 2048
EXT = 2304
TOKQ = 512
EXTQ = 768
INTER = 4096
NE = 8
DK = 64
SCALE = 1.0 / np.sqrt(DK * 2.0)
TEMP = 1.5
GTC = float(np.sqrt(2.0 / np.pi))

_STAGE = "full"


class Sem:
    def __init__(self, handle):
        self.h = handle
        self.n = 0

    def bump(self, k=1):
        self.n += k
        return (self, k)

    def val(self):
        return (self, self.n)


class B:
    ENGINES = ("tensor", "vector", "scalar", "gpsimd", "sync")

    def __init__(self, nc):
        self.nc = nc
        self.prog = {e: [] for e in self.ENGINES}

    def op(self, engine, method, *args, waits=(), incs=(), **kw):
        waits = tuple(w for w in waits if w is not None)

        def f(eng, method=method, args=args, kw=kw, waits=waits, incs=tuple(incs)):
            for (s, v) in waits:
                eng.wait_ge(s.h, v)
            inst = getattr(eng, method)(*args, **kw)
            for (s, v) in incs:
                inst = inst.then_inc(s.h, v)
        self.prog[engine].append(f)


def ts128(i):
    return slice(i * 128, (i + 1) * 128)


def build_program(stage):
    nc = bass.Bass("TRN2", target_bir_lowering=False)
    b = B(nc)

    def din(name, shape, dt=F32):
        return nc.dram_tensor(name, shape, dt, kind="ExternalInput").ap()

    xT = din("xT", [H, EXT])
    Wret = din("Wret", [H, H])
    W1a = din("W1a", [H, INTER])
    W2a = din("W2a", [INTER, H])
    w1s = din("w1s", [H, INTER])
    w2s = din("w2s", [H, INTER])
    w3s = din("w3s", [INTER, H])
    W1e = din("W1e", [NE, H, INTER])
    W2e = din("W2e", [NE, H, INTER])
    W3e = din("W3e", [NE, INTER, H])
    Wg = din("Wg", [H, 128])
    WqT = din("WqT", [H, H])
    WkT = din("WkT", [H, H])
    WvT = din("WvT", [H, H])
    WoT = din("WoT", [H, H])

    biaspk = din("biaspk", [128, 128])
    b1a_in = din("b1a_in", [128, 32])
    bv_in = din("bv_in", [1, H])
    kadj_in = din("kadj_in", [128, 32])
    extmask_in = din("extmask_in", [128, 256])
    identf_in = din("identf_in", [128, 128])
    selabT_in = din("selabT_in", [128, 2], BF16)
    selab2_in = din("selab2_in", [2, 128])

    out_T = nc.dram_tensor("out_T", [H, TOK], F32, kind="ExternalOutput").ap()

    rowscr = nc.dram_tensor("rowscr", [8, 4096], F32, kind="Internal").ap()
    maskrow_scr = nc.dram_tensor("maskrow_scr", [NE, TOK], BF16, kind="Internal").ap()
    maskscr = nc.dram_tensor("maskscr", [32, 128, TOK], BF16, kind="Internal").ap()
    qh_scr = nc.dram_tensor("qh_scr", [NT, 128, TOK], BF16, kind="Internal").ap()
    cc_in = nc.dram_tensor("cc_in", [H, TOK], BF16, kind="Internal")
    cc_out = nc.dram_tensor("cc_out", [2 * H, TOK], BF16, kind="Internal")

    top = ExitStack()

    def sb(name, shape, dt):
        return top.enter_context(nc.sbuf_tensor(name, shape, dt))

    sems = {n: Sem(top.enter_context(nc.semaphore(n)))
            for n in ["qW", "qA", "PE", "DVE", "ACT", "POOL", "CC"]}
    qW, qA = sems["qW"], sems["qA"]
    PE, DVE, ACT, POOL, CC = sems["PE"], sems["DVE"], sems["ACT"], sems["POOL"], sems["CC"]

    PS8 = top.enter_context(nc.psum_tensor("PS8", [128, 4096], F32))
    bankfree = [None] * 8

    def bwaits(lo, hi):
        seen = []
        for i in range(lo // 512, (hi + 511) // 512):
            w = bankfree[i]
            if w is not None and w not in seen:
                seen.append(w)
        return seen

    def bfree(lo, hi, semval):
        for i in range(lo // 512, (hi + 511) // 512):
            bankfree[i] = semval

    XR = sb("XR", [128, NT * EXT], F32)
    ROWS = sb("ROWS", [1, 4608], F32)
    ONESC = sb("ONESC", [128, 1], BF16)
    ONESRB = sb("ONESRB", [1, 128], BF16)
    ONESRF = sb("ONESRF", [1, 128], F32)
    KADJ = sb("KADJ", [128, 32], F32)
    IDF = sb("IDF", [128, 128], F32)
    SELABT = sb("SELABT", [128, 2], BF16)
    SELAB2 = sb("SELAB2", [2, 128], F32)
    BIAS = sb("BIAS", [128, 128], F32)
    B1A = sb("B1A", [128, 32], F32)
    EXTM = sb("EXTM", [128, 256], F32)
    CONSTS = sb("CONSTS", [128, 4], F32)
    SQC = [sb(f"SQC{i}", [128, 512], BF16) for i in range(2)]
    FST = [sb(f"FST{i}", [128, 1024], F32) for i in range(2)]
    WBB = [sb(f"WBB{i}", [128, 1024], BF16) for i in range(4)]
    fst_free = [None, None]
    wfree = [None] * 4
    wready = [None] * 4
    fst_ctr = [0]

    def bcol(i):
        return BIAS[:, i:i + 1]

    def xr_own(t):
        return XR[:, t * EXT + 256: (t + 1) * EXT]

    def stream_cols(dram_w, col, bslot, row0=0, ntiles=NT):
        fs = fst_ctr[0] % 2
        fst_ctr[0] += 1
        src = dram_w[row0 * 128:(row0 + ntiles) * 128, col * 128:(col + 1) * 128]
        b.op("sync", "dma_start",
             FST[fs][:, 0:ntiles * 128].rearrange("p (t n) -> p t n", t=ntiles),
             src.rearrange("(t p) n -> p t n", p=128),
             waits=[fst_free[fs]], incs=[qW.bump(16)])
        lw = qW.val()
        b.op("scalar", "activation", WBB[bslot][:, 0:ntiles * 128],
             FST[fs][:, 0:ntiles * 128], AF.Copy, waits=[lw, wfree[bslot]],
             incs=[ACT.bump()])
        wready[bslot] = ACT.val()
        fst_free[fs] = ACT.val()

    def stream_rows(dram_w, row0, bslot, ncols=H):
        fs = fst_ctr[0] % 2
        fst_ctr[0] += 1
        src = dram_w[row0 * 128:(row0 + 1) * 128, 0:ncols]
        b.op("sync", "dma_start", FST[fs][:, 0:ncols], src,
             waits=[fst_free[fs]], incs=[qW.bump(16)])
        lw = qW.val()
        b.op("scalar", "activation", WBB[bslot][:, 0:ncols], FST[fs][:, 0:ncols],
             AF.Copy, waits=[lw, wfree[bslot]], incs=[ACT.bump()])
        wready[bslot] = ACT.val()
        fst_free[fs] = ACT.val()

    b.op("sync", "dma_start", XR[:].rearrange("p (t n) -> p t n", t=NT),
         xT.rearrange("(t p) n -> p t n", p=128), incs=[qW.bump(16)])
    b.op("sync", "dma_start", BIAS[:], biaspk, incs=[qW.bump(16)])
    b.op("sync", "dma_start", B1A[:], b1a_in, incs=[qW.bump(16)])
    b.op("sync", "dma_start", KADJ[:], kadj_in, incs=[qW.bump(16)])
    b.op("sync", "dma_start", IDF[:], identf_in, incs=[qW.bump(16)])
    b.op("sync", "dma_start", SELABT[:], selabT_in, incs=[qW.bump(16)])
    b.op("sync", "dma_start", SELAB2[:], selab2_in, incs=[qW.bump(16)])
    b.op("sync", "dma_start", EXTM[:], extmask_in, incs=[qW.bump(16)])
    loads = qW.val()

    b.op("vector", "memset", ONESC[:], 1.0, incs=[DVE.bump()])
    b.op("vector", "memset", ONESRB[:], 1.0, incs=[DVE.bump()])
    b.op("vector", "memset", ONESRF[:], 1.0, incs=[DVE.bump()])
    b.op("vector", "memset", CONSTS[:, 0:1], GTC, incs=[DVE.bump()])
    b.op("vector", "memset", CONSTS[:, 1:2], SCALE / TEMP, incs=[DVE.bump()])
    b.op("vector", "memset", CONSTS[:, 2:3], 1e-5, incs=[DVE.bump()])
    consts_dve = DVE.val()
    b.op("vector", "tensor_scalar", BIAS[:, 72:80], BIAS[:, 64:72], 0.5, None,
         ALU.mult, waits=[loads], incs=[DVE.bump()])

    def emit_ln(ncols, src, dst, bc_mu, bc_r, tmpf, extra_first_waits=()):
        nch = (ncols + 511) // 512
        cast_done = []
        for t in range(NT):
            b.op("scalar", "activation", dst(t), src(t), AF.Copy,
                 waits=list(extra_first_waits) if t == 0 else [], incs=[ACT.bump()])
            cast_done.append(ACT.val())
        slot_last = [None, None]
        for c in range(nch):
            w = min(512, ncols - c * 512)
            psq = PS8[0:1, 0:512]
            psx = PS8[0:1, 2048:2560]
            for t in range(NT):
                sl = SQC[t % 2]
                b.op("scalar", "activation", sl[:, 0:w], dst(t)[:, c * 512:c * 512 + w],
                     AF.Square, waits=[slot_last[t % 2], cast_done[t]],
                     incs=[ACT.bump()])
                aw = ACT.val()
                b.op("tensor", "matmul", psq[0:1, 0:w], ONESC[:], sl[:, 0:w],
                     start=(t == 0), stop=(t == NT - 1),
                     waits=[aw, consts_dve] + (bwaits(0, 512) if t == 0 else []),
                     incs=[PE.bump()])
                slot_last[t % 2] = PE.val()
                b.op("tensor", "matmul", psx[0:1, 0:w], ONESC[:],
                     dst(t)[:, c * 512:c * 512 + w],
                     start=(t == 0), stop=(t == NT - 1),
                     waits=(bwaits(2048, 2560) if t == 0 else []),
                     incs=[PE.bump()] if t == NT - 1 else [])
            pedone = PE.val()
            mu = ROWS[0:1, c * 512:c * 512 + w]
            rr = ROWS[0:1, 2048 + c * 512:2048 + c * 512 + w]
            t1 = ROWS[0:1, 4096:4096 + w]
            b.op("vector", "tensor_scalar", mu, psx[0:1, 0:w], 1.0 / H, None,
                 ALU.mult, waits=[pedone], incs=[DVE.bump()])
            b.op("vector", "tensor_scalar", rr, psq[0:1, 0:w], 1.0 / H, None,
                 ALU.mult, incs=[DVE.bump()])
            dv = DVE.val()
            bfree(0, 512, dv)
            bfree(2048, 2560, dv)
            b.op("vector", "tensor_tensor", t1, mu, mu, ALU.mult, waits=[DVE.val()],
                 incs=[DVE.bump()])
            b.op("vector", "tensor_tensor", rr, rr, t1, ALU.subtract,
                 waits=[DVE.val()], incs=[DVE.bump()])
            b.op("scalar", "activation", rr, rr, AF.Sqrt, bias=CONSTS[0:1, 2:3], waits=[DVE.val()],
                 incs=[ACT.bump()])
            b.op("vector", "reciprocal", rr, rr, waits=[ACT.val()], incs=[DVE.bump()])
        rows_done = DVE.val()
        for c in range(nch):
            w = min(512, ncols - c * 512)
            pmu = PS8[:, 512:512 + 512]
            prr = PS8[:, 2560:2560 + 512]
            b.op("tensor", "matmul", pmu[:, 0:w], ONESRF[:],
                 ROWS[0:1, c * 512:c * 512 + w], start=True, stop=True,
                 waits=[rows_done] + bwaits(512, 1024), incs=[PE.bump()])
            b.op("tensor", "matmul", prr[:, 0:w], ONESRF[:],
                 ROWS[0:1, 2048 + c * 512:2048 + c * 512 + w], start=True, stop=True,
                 waits=bwaits(2560, 3072), incs=[PE.bump()])
            pbc = PE.val()
            b.op("vector", "tensor_copy", bc_mu[:, c * 512:c * 512 + w], pmu[:, 0:w],
                 waits=[pbc], incs=[DVE.bump()])
            b.op("vector", "tensor_copy", bc_r[:, c * 512:c * 512 + w], prr[:, 0:w],
                 incs=[DVE.bump()])
            dvv = DVE.val()
            bfree(512, 1024, dvv)
            bfree(2560, 3072, dvv)
        bc = DVE.val()
        for t in range(NT):
            b.op("vector", "tensor_tensor", tmpf[:, 0:ncols], src(t), bc_mu[:, 0:ncols],
                 ALU.subtract, incs=[DVE.bump()])
            b.op("vector", "tensor_tensor", dst(t), tmpf[:, 0:ncols], bc_r[:, 0:ncols],
                 ALU.mult, incs=[DVE.bump()])
        return DVE.val()

    with ExitStack() as ph:
        def psb(name, shape, dt):
            return ph.enter_context(nc.sbuf_tensor(name, shape, dt))

        XE = psb("XE", [128, NT * EXTQ], F32)
        HALO = psb("HALO", [128, NT * 256], F32)
        XLN = psb("XLN", [128, NT * EXTQ], BF16)
        BC0 = psb("BC0", [128, EXTQ], F32)
        BC1 = psb("BC1", [128, EXTQ], F32)
        TMPF = psb("TMPF", [128, EXTQ], F32)
        H1 = psb("H1", [128, NT * TOKQ], F32)
        HB = psb("HB", [128, NT * TOKQ], BF16)
        SIG = psb("SIG", [128, EXTQ], F32)
        RET = psb("RET", [128, EXTQ], F32)
        HALF = psb("HALF", [128, EXTQ], F32)
        GA = psb("GA", [128, 4 * TOKQ], BF16)
        YT = psb("YT", [128, TOKQ], F32)
        SQT = psb("SQT", [128, TOKQ], F32)
        TMP2 = psb("TMP2", [128, TOKQ], F32)

        b.op("vector", "memset", HALF[:], 0.5, incs=[DVE.bump()])

        gaf = [None]
        for p4 in range(4):
            def xev(t):
                return XE[:, t * EXTQ:(t + 1) * EXTQ]

            def xeo(t):
                return XE[:, t * EXTQ + 256:(t + 1) * EXTQ]

            def xro(t, p4=p4):
                return XR[:, t * EXT + 256 + p4 * TOKQ: t * EXT + 256 + (p4 + 1) * TOKQ]

            def xlnv(t):
                return XLN[:, t * EXTQ:(t + 1) * EXTQ]

            def h1v(t):
                return H1[:, t * TOKQ:(t + 1) * TOKQ]

            def hbv(t):
                return HB[:, t * TOKQ:(t + 1) * TOKQ]

            for t in range(NT):
                if p4 == 0:
                    b.op("vector", "tensor_copy", XE[:, t * EXTQ:t * EXTQ + 256],
                         XR[:, t * EXT:t * EXT + 256],
                         waits=[loads] if t == 0 else [], incs=[DVE.bump()])
                else:
                    b.op("vector", "tensor_copy", XE[:, t * EXTQ:t * EXTQ + 256],
                         HALO[:, t * 256:(t + 1) * 256], incs=[DVE.bump()])
                b.op("vector", "tensor_copy", XE[:, t * EXTQ + 256:(t + 1) * EXTQ],
                     xro(t), incs=[DVE.bump()])
            xe_done = DVE.val()

            ln0_done = emit_ln(EXTQ, xev, xlnv, BC0, BC1, TMPF,
                               extra_first_waits=[xe_done])

            stream_cols(Wret, 0, 0)
            scan_done = None
            for ko in range(NT):
                sl = ko % 2
                if ko + 1 < NT:
                    stream_cols(Wret, ko + 1, (ko + 1) % 2)
                for hh in range(NT):
                    wv0 = [wready[sl], ln0_done] + (bwaits(0, EXTQ) if hh == 0 else [])
                    for (off, w) in ((0, 512), (512, 256)):
                        b.op("tensor", "matmul", PS8[:, off:off + w],
                             WBB[sl][:, ts128(hh)], xlnv(hh)[:, off:off + w],
                             start=(hh == 0), stop=(hh == NT - 1),
                             waits=wv0 if (hh == 0 and off == 0) else [],
                             incs=[PE.bump()] if (hh == NT - 1 and off == 512) else [])
                pe_sig = PE.val()
                wfree[sl] = pe_sig
                b.op("scalar", "activation", SIG[:], PS8[:, 0:EXTQ], AF.Sigmoid,
                     bias=bcol(0 + ko), waits=[pe_sig, scan_done], incs=[ACT.bump()])
                bfree(0, EXTQ, ACT.val())
                if p4 == 0:
                    b.op("vector", "tensor_tensor", SIG[:, 0:256], SIG[:, 0:256],
                         EXTM[:], ALU.mult, waits=[ACT.val()], incs=[DVE.bump()])
                b.op("vector", "tensor_tensor_scan", RET[:], SIG[:], HALF[:], 0.0,
                     ALU.add, ALU.mult, waits=[ACT.val()], incs=[DVE.bump()])
                scan_done = DVE.val()
                b.op("vector", "tensor_tensor", TMPF[:, 0:TOKQ], xeo(ko),
                     BC0[:, 256:EXTQ], ALU.subtract, incs=[DVE.bump()])
                b.op("vector", "tensor_tensor", TMPF[:, 0:TOKQ], TMPF[:, 0:TOKQ],
                     BC1[:, 256:EXTQ], ALU.mult, incs=[DVE.bump()])
                b.op("vector", "tensor_tensor", h1v(ko), TMPF[:, 0:TOKQ],
                     RET[:, 256:EXTQ], ALU.add, incs=[DVE.bump()])

            emit_ln(TOKQ, h1v, hbv, BC0, BC1, TMPF,
                    extra_first_waits=[DVE.val()])
            for t in range(NT):
                b.op("vector", "tensor_tensor", TMPF[:, 0:TOKQ], h1v(t), BC0[:, 0:TOKQ],
                     ALU.subtract, incs=[DVE.bump()])
                b.op("vector", "tensor_tensor", TMPF[:, 0:TOKQ], TMPF[:, 0:TOKQ],
                     BC1[:, 0:TOKQ], ALU.mult, incs=[DVE.bump()])
                b.op("vector", "tensor_scalar", hbv(t), TMPF[:, 0:TOKQ],
                     bcol(8 + t), bcol(16 + t), ALU.mult, ALU.add, incs=[DVE.bump()])
                b.op("vector", "tensor_scalar", h1v(t), TMPF[:, 0:TOKQ],
                     bcol(8 + t), bcol(16 + t), ALU.mult, ALU.add, incs=[DVE.bump()])
            for t in range(NT):
                b.op("vector", "tensor_scalar", h1v(t), h1v(t), bcol(24 + t), None,
                     ALU.add, incs=[DVE.bump()])

            TT2 = TMPF[:, 0:TOKQ]
            hb_done = DVE.val()
            for grp in range(8):
                for ibl in range(4):
                    ib = grp * 4 + ibl
                    sl = ib % 2
                    stream_cols(W1a, ib, sl)
                    lo = 0 if ibl % 2 == 0 else 512
                    ups = PS8[:, lo:lo + 512]
                    for hh in range(NT):
                        wv = ([wready[sl], hb_done] + bwaits(lo, lo + 512)) \
                            if hh == 0 else []
                        b.op("tensor", "matmul", ups, WBB[sl][:, ts128(hh)], hbv(hh),
                             start=(hh == 0), stop=(hh == NT - 1),
                             waits=wv, incs=[PE.bump()] if hh == NT - 1 else [])
                    peu = PE.val()
                    wfree[sl] = peu
                    gv = GA[:, ibl * TOKQ:(ibl + 1) * TOKQ]
                    b.op("vector", "tensor_scalar", YT[:], ups,
                         B1A[:, ib:ib + 1], None, ALU.add, waits=[peu],
                         incs=[DVE.bump()])
                    bfree(lo, lo + 512, DVE.val())
                    b.op("scalar", "activation", SQT[:], YT[:], AF.Square,
                         waits=[DVE.val()], incs=[ACT.bump()])
                    b.op("vector", "tensor_scalar", SQT[:], SQT[:], 0.044715, 1.0,
                         ALU.mult, ALU.add, waits=[ACT.val()], incs=[DVE.bump()])
                    b.op("vector", "tensor_tensor", SQT[:], YT[:], SQT[:], ALU.mult,
                         incs=[DVE.bump()])
                    b.op("scalar", "activation", TT2[:], SQT[:], AF.Tanh, scale=CONSTS[:, 0:1],
                         waits=[DVE.val()], incs=[ACT.bump()])
                    b.op("vector", "tensor_scalar", TT2[:], TT2[:], 0.5, 0.5,
                         ALU.mult, ALU.add, waits=[ACT.val()], incs=[DVE.bump()])
                    b.op("vector", "tensor_tensor", gv, YT[:], TT2[:], ALU.mult,
                         waits=[gaf[0]], incs=[DVE.bump()])
                g_ready = DVE.val()
                for ct in range(NT):
                    sl = 2 + (ct % 2)
                    stream_cols(W2a, ct, sl, row0=grp * 4, ntiles=4)
                    lo = 1024 + (ct % 2) * 512
                    dn = PS8[:, lo:lo + 512]
                    for ibl in range(4):
                        wv = ([wready[sl], g_ready] + bwaits(lo, lo + 512)) \
                            if ibl == 0 else []
                        b.op("tensor", "matmul", dn, WBB[sl][:, ts128(ibl)],
                             GA[:, ibl * TOKQ:(ibl + 1) * TOKQ],
                             start=(ibl == 0), stop=(ibl == 3),
                             waits=wv, incs=[PE.bump()] if ibl == 3 else [])
                    ped = PE.val()
                    wfree[sl] = ped
                    b.op("vector", "tensor_tensor", h1v(ct), h1v(ct), dn, ALU.add,
                         waits=[ped], incs=[DVE.bump()])
                    bfree(lo, lo + 512, DVE.val())
                gaf[0] = PE.val()

            if p4 < 3:
                for t in range(NT):
                    b.op("vector", "tensor_copy", HALO[:, t * 256:(t + 1) * 256],
                         XR[:, t * EXT + 256 + p4 * TOKQ + 256:
                            t * EXT + 256 + (p4 + 1) * TOKQ], incs=[DVE.bump()])

            emit_ln(TOKQ, h1v, hbv, BC0, BC1, TMPF,
                    extra_first_waits=[DVE.val()])
            for t in range(NT):
                b.op("vector", "tensor_tensor", TMPF[:, 0:TOKQ], h1v(t), BC0[:, 0:TOKQ],
                     ALU.subtract, incs=[DVE.bump()])
                b.op("vector", "tensor_tensor", TMPF[:, 0:TOKQ], TMPF[:, 0:TOKQ],
                     BC1[:, 0:TOKQ], ALU.mult, incs=[DVE.bump()])
                b.op("vector", "tensor_scalar", TMP2[:], TMPF[:, 0:TOKQ],
                     bcol(32 + t), bcol(40 + t), ALU.mult, ALU.add, incs=[DVE.bump()])
                b.op("vector", "tensor_tensor", xro(t), xro(t), TMP2[:], ALU.add,
                     incs=[DVE.bump()])

    def emit_ffn(tag):
        fence = [x.val() for x in (PE, DVE, ACT, POOL, qA, qW)]
        with ExitStack() as ph:
            def psb(name, shape, dt):
                return ph.enter_context(nc.sbuf_tensor(tag + name, shape, dt))

            XLF = psb("XLF", [128, NT * TOK], BF16)

            def xlf(t):
                return XLF[:, t * TOK:(t + 1) * TOK]

            with ExitStack() as lnscope:
                BC0 = lnscope.enter_context(nc.sbuf_tensor(tag + "BC0", [128, TOK], F32))
                BC1 = lnscope.enter_context(nc.sbuf_tensor(tag + "BC1", [128, TOK], F32))
                TMPF = lnscope.enter_context(nc.sbuf_tensor(tag + "TMPF", [128, TOK], F32))
                emit_ln(TOK, xr_own, xlf, BC0, BC1, TMPF,
                        extra_first_waits=fence)
            xlf_done = DVE.val()

            LTM = psb("LTM", [128, 128], F32)
            M16 = psb("M16", [128, 16], F32)
            TK = psb("TK", [128, 16], F32)
            EQ = psb("EQ", [128, 16], F32)
            TMQ = psb("TMQ", [128, 16], F32)
            MTM = psb("MTM", [128, 128], F32)
            MTB = psb("MTB", [128, 128], BF16)

            stream_cols(Wg, 0, 0)
            wg_ready = wready[0]

            lg = PS8[0:128, 0:128]
            for tt in range(16):
                for hh in range(NT):
                    wv = []
                    if tt == 0 and hh == 0:
                        wv = [wg_ready, xlf_done] + bwaits(0, 512)
                    b.op("tensor", "matmul", lg[:, tt * 8:(tt + 1) * 8],
                         xlf(hh)[:, ts128(tt)], WBB[0][:, hh * 128:hh * 128 + 8],
                         start=(hh == 0), stop=(hh == NT - 1),
                         waits=wv,
                         incs=[PE.bump()] if (tt == 15 and hh == NT - 1) else [])
            pel = PE.val()
            wfree[0] = pel
            b.op("vector", "tensor_copy", LTM[:], lg, waits=[pel], incs=[DVE.bump()])
            bfree(0, 512, DVE.val())
            b.op("vector", "tensor_reduce", M16[:],
                 LTM[:].rearrange("p (t e) -> p t e", e=8),
                 mybir.AxisListType.X, ALU.max, waits=[DVE.val()],
                 incs=[DVE.bump()])
            b.op("vector", "memset", TK[:], 0.0, waits=[DVE.val()],
                 incs=[DVE.bump()])
            for e in range(8):
                b.op("vector", "tensor_tensor", EQ[:], LTM[:, e:e + 121:8], M16[:],
                     ALU.is_equal, waits=[DVE.val()], incs=[DVE.bump()])
                b.op("vector", "tensor_scalar", TMQ[:], TK[:], -1.0, 1.0,
                     ALU.mult, ALU.add, waits=[DVE.val()], incs=[DVE.bump()])
                b.op("vector", "tensor_tensor", MTM[:, e * 16:(e + 1) * 16], EQ[:],
                     TMQ[:], ALU.mult, waits=[DVE.val()], incs=[DVE.bump()])
                b.op("vector", "tensor_tensor", TK[:], TK[:],
                     MTM[:, e * 16:(e + 1) * 16], ALU.add, waits=[DVE.val()],
                     incs=[DVE.bump()])
            masks_dve = DVE.val()
            tp = PS8[0:128, 512:640]
            b.op("tensor", "transpose", tp, MTM[:], IDF[:],
                 waits=[masks_dve, loads] + bwaits(512, 1024), incs=[PE.bump()])
            b.op("scalar", "activation", MTB[:], tp, AF.Copy, waits=[PE.val()],
                 incs=[ACT.bump()])
            bfree(512, 1024, ACT.val())
            mtb_act = ACT.val()
            for e in range(8):
                b.op("gpsimd", "dma_start",
                     maskrow_scr[e].rearrange("(c t) -> c t", c=16),
                     MTB[e * 16:(e + 1) * 16, :], waits=[mtb_act] if e == 0 else [],
                     incs=[qA.bump(16)])
            maskrows_out = qA.val()

            if stage == "dbgffn":
                DBGF = psb("DBGF", [128, TOK], F32)
                DBG2 = psb("DBG2", [128, 128], F32)
                b.op("scalar", "activation", DBG2[:], PS8[0:128, 0:128], AF.Copy,
                     waits=[maskrows_out], incs=[ACT.bump()])
                b.op("gpsimd", "dma_start", out_T[512:640, 0:128], DBG2[:],
                     waits=[ACT.val()], incs=[qA.bump(16)])
                b.op("gpsimd", "dma_start", out_T[640:768, 0:128], WBB[0][:, 0:128],
                     incs=[qA.bump(16)])
                b.op("gpsimd", "dma_start", out_T[768:896, 0:16], M16[:],
                     incs=[qA.bump(16)])
                b.op("gpsimd", "dma_start", out_T[768:896, 16:32], TK[:],
                     incs=[qA.bump(16)])
                b.op("gpsimd", "dma_start", out_T[768:896, 32:48], EQ[:],
                     incs=[qA.bump(16)])
                b.op("gpsimd", "dma_start", out_T[768:896, 48:64], TMQ[:],
                     incs=[qA.bump(16)])
                b.op("scalar", "activation", DBGF[:], xlf(0), AF.Copy,
                     waits=[maskrows_out], incs=[ACT.bump()])
                b.op("gpsimd", "dma_start", out_T[256:384, 0:2048], DBGF[:],
                     waits=[ACT.val()], incs=[qA.bump(16)])

                b.op("gpsimd", "dma_start", out_T[0:128, 0:128], LTM[:],
                     waits=[maskrows_out], incs=[qA.bump(16)])
                b.op("gpsimd", "dma_start", out_T[0:128, 128:256], MTM[:],
                     incs=[qA.bump(16)])
                b.op("gpsimd", "dma_start", out_T[128:136, 0:2048],
                     maskrow_scr[:, :], incs=[qA.bump(16)])
                return DVE.val()

            G = psb("G", [128, 8 * TOK], BF16)
            TG = psb("TG", [128, 1024], BF16)
            MKB = [psb(f"MKB{i}", [128, TOK], BF16) for i in range(2)]
            mkb_ready = [None, None]
            mkb_free = [None, None]

            def prefetch_mask(e):
                slm = e % 2
                b.op("gpsimd", "dma_start", MKB[slm][:],
                     maskrow_scr[e:e + 1, :].to_broadcast([128, TOK]),
                     waits=[maskrows_out, mkb_free[slm]], incs=[qA.bump(16)])
                mkb_ready[slm] = qA.val()

            prefetch_mask(0)
            prefetch_mask(1)

            ga_free = None
            for ei in range(9):
                e = ei - 1
                msl = e % 2
                for q in range(4):
                    for il in range(8):
                        ib = q * 8 + il
                        if il == 0:
                            for w in range(2):
                                wsl = (il * 2 + w) % 4
                                if e < 0:
                                    stream_cols(w1s if w == 0 else w2s, ib, wsl)
                                else:
                                    stream_cols(W1e[e] if w == 0 else W2e[e], ib, wsl)
                        if il + 1 < 8:
                            ib2 = q * 8 + il + 1
                            for w in range(2):
                                wsl = ((il + 1) * 2 + w) % 4
                                if e < 0:
                                    stream_cols(w1s if w == 0 else w2s, ib2, wsl)
                                else:
                                    stream_cols(W1e[e] if w == 0 else W2e[e], ib2, wsl)
                        for th in range(2):
                            lo = th * 2048
                            for w in range(2):
                                wsl = (il * 2 + w) % 4
                                ups = PS8[:, lo + w * 1024: lo + (w + 1) * 1024]
                                for hh in range(NT):
                                    wv = []
                                    if hh == 0:
                                        wv = [wready[wsl]] + bwaits(
                                            lo + w * 1024, lo + (w + 1) * 1024)
                                    for cc in range(2):
                                        b.op("tensor", "matmul",
                                             ups[:, cc * 512:(cc + 1) * 512],
                                             WBB[wsl][:, ts128(hh)],
                                             xlf(hh)[:, th * 1024 + cc * 512:
                                                     th * 1024 + (cc + 1) * 512],
                                             start=(hh == 0), stop=(hh == NT - 1),
                                             waits=wv if cc == 0 else [],
                                             incs=[PE.bump()]
                                             if (hh == NT - 1 and cc == 1) else [])
                            peu = PE.val()
                            for w in range(2):
                                wfree[(il * 2 + w) % 4] = peu
                            lo_a = th * 2048
                            lo_b = th * 2048 + 1024
                            gv = G[:, il * TOK + th * 1024: il * TOK + (th + 1) * 1024]
                            b.op("scalar", "activation", TG[:], PS8[:, lo_a:lo_a + 1024],
                                 AF.Gelu, waits=[peu], incs=[ACT.bump()])
                            bfree(lo_a, lo_a + 1024, ACT.val())
                            b.op("vector", "tensor_tensor", gv, TG[:],
                                 PS8[:, lo_b:lo_b + 1024], ALU.mult,
                                 waits=[ACT.val(), ga_free], incs=[DVE.bump()])
                            bfree(lo_b, lo_b + 1024, DVE.val())
                            if e >= 0:
                                b.op("vector", "tensor_tensor", gv, gv,
                                     MKB[msl][:, th * 1024:(th + 1) * 1024], ALU.mult,
                                     waits=[mkb_ready[msl]], incs=[DVE.bump()])
                    gq_ready = DVE.val()
                    if e >= 0 and q == 3:
                        mkb_free[msl] = gq_ready
                    for ct in range(NT):
                        wsl = ct % 4
                        if e < 0:
                            stream_cols(w3s, ct, wsl, row0=q * 8, ntiles=8)
                        else:
                            stream_cols(W3e[e], ct, wsl, row0=q * 8, ntiles=8)
                        lo = (ct % 2) * 2048
                        dn = PS8[:, lo:lo + 2048]
                        for il in range(8):
                            wv = []
                            if il == 0:
                                wv = [wready[wsl], gq_ready] + bwaits(lo, lo + 2048)
                            for ch in range(4):
                                b.op("tensor", "matmul", dn[:, ch * 512:(ch + 1) * 512],
                                     WBB[wsl][:, ts128(il)],
                                     G[:, il * TOK + ch * 512: il * TOK + (ch + 1) * 512],
                                     start=(il == 0), stop=(il == 7),
                                     waits=wv if ch == 0 else [],
                                     incs=[PE.bump()]
                                     if (il == 7 and ch == 3) else [])
                        ped = PE.val()
                        wfree[wsl] = ped
                        b.op("vector", "tensor_tensor", xr_own(ct), xr_own(ct), dn[:],
                             ALU.add, waits=[ped], incs=[DVE.bump()])
                        bfree(lo, lo + 2048, DVE.val())
                    ga_free = PE.val()
                if 0 <= e + 2 < 8:
                    prefetch_mask(e + 2)

    if stage in ("dbgffn", "ffn1", "attn_b", "full"):
        emit_ffn("f1")

    if stage in ("attn_b", "full"):
        fence = [x.val() for x in (PE, DVE, ACT, POOL, qA, qW)]
        with ExitStack() as ph:
            def psb(name, shape, dt):
                return ph.enter_context(nc.sbuf_tensor("ab" + name, shape, dt))

            with ExitStack() as qscope:
                def qsb(name, shape, dt):
                    return qscope.enter_context(nc.sbuf_tensor("aq" + name, shape, dt))

                XL2 = qsb("XL2", [128, NT * TOK], BF16)
                BC0 = qsb("BC0", [128, TOK], F32)
                BC1 = qsb("BC1", [128, TOK], F32)
                TMPF = qsb("TMPF", [128, TOK], F32)
                SQB = qsb("SQB", [128, TOK], BF16)
                NRB = qsb("NRB", [128, TOK], F32)
                QHB = qsb("QHB", [128, TOK], BF16)
                SS2 = qsb("SS2", [2, TOK], F32)

                def xl2(t):
                    return XL2[:, t * TOK:(t + 1) * TOK]

                ln2_done = emit_ln(TOK, xr_own, xl2, BC0, BC1, TMPF,
                                   extra_first_waits=fence)
                b.op("gpsimd", "dma_start",
                     cc_in.ap().rearrange("(t p) n -> p t n", p=128),
                     XL2[:].rearrange("p (t n) -> p t n", t=NT), waits=[ln2_done],
                     incs=[qA.bump(16)])
                b.op("gpsimd", "collective_compute", "AllGather", ALU.bypass,
                     replica_groups=[[0, 1], [2, 3], [4, 5], [6, 7]],
                     ins=[cc_in.ap().opt()], outs=[cc_out.ap().opt()],
                     waits=[qA.val()], incs=[CC.bump()])
                cc_done = CC.val()

                for pair in range(NT):
                    wsl = pair % 4
                    stream_cols(WqT, pair, wsl)
                    lo = (pair % 2) * 2048
                    rowb = 2048 if pair % 2 == 0 else 0
                    qp = PS8[:, lo:lo + 2048]
                    for hh in range(NT):
                        wv = []
                        if hh == 0:
                            wv = [wready[wsl], ln2_done] + bwaits(lo, lo + 2048)
                        for ch in range(4):
                            b.op("tensor", "matmul", qp[:, ch * 512:(ch + 1) * 512],
                                 WBB[wsl][:, ts128(hh)],
                                 xl2(hh)[:, ch * 512:(ch + 1) * 512],
                                 start=(hh == 0), stop=(hh == NT - 1),
                                 waits=wv if ch == 0 else [],
                                 incs=[PE.bump()] if (hh == NT - 1 and ch == 3) else [])
                    pq = PE.val()
                    wfree[wsl] = pq
                    b.op("scalar", "activation", SQB[:], qp, AF.Square,
                         bias=bcol(48 + pair), waits=[pq], incs=[ACT.bump()])
                    aq = ACT.val()
                    sp = PS8[0:2, rowb:rowb + 512]
                    for ch in range(4):
                        b.op("tensor", "matmul", sp, SELABT[:],
                             SQB[:, ch * 512:(ch + 1) * 512], start=True, stop=True,
                             waits=[aq, consts_dve] + bwaits(rowb, rowb + 512),
                             incs=[PE.bump()])
                        b.op("vector", "tensor_copy", SS2[0:2, ch * 512:(ch + 1) * 512],
                             sp, waits=[PE.val()], incs=[DVE.bump()])
                        bfree(rowb, rowb + 512, DVE.val())
                    b.op("scalar", "activation", SS2[0:2, :], SS2[0:2, :], AF.Sqrt,
                         waits=[DVE.val()], incs=[ACT.bump()])
                    b.op("vector", "tensor_scalar", SS2[0:2, :], SS2[0:2, :], 1e-8,
                         None, ALU.max, waits=[ACT.val()], incs=[DVE.bump()])
                    b.op("vector", "reciprocal", SS2[0:2, :], SS2[0:2, :],
                         incs=[DVE.bump()])
                    rd = DVE.val()
                    for ch in range(4):
                        pb = PS8[:, rowb:rowb + 512]
                        b.op("tensor", "matmul", pb, SELAB2[:],
                             SS2[0:2, ch * 512:(ch + 1) * 512], start=True, stop=True,
                             waits=[rd] + bwaits(rowb, rowb + 512), incs=[PE.bump()])
                        b.op("vector", "tensor_copy", NRB[:, ch * 512:(ch + 1) * 512],
                             pb, waits=[PE.val()], incs=[DVE.bump()])
                        bfree(rowb, rowb + 512, DVE.val())
                    nb = DVE.val()
                    b.op("vector", "scalar_tensor_tensor", QHB[:], qp, bcol(48 + pair),
                         NRB[:], ALU.add, ALU.mult, waits=[nb], incs=[DVE.bump()])
                    bfree(lo, lo + 2048, DVE.val())
                    b.op("gpsimd", "dma_start", qh_scr[pair], QHB[:],
                         waits=[DVE.val()], incs=[qA.bump(16)])
                qhat_out = qA.val()

            with ExitStack() as mscope:
                QROW = mscope.enter_context(nc.sbuf_tensor("abQROW", [128, TOK], F32))
                MGEN = [mscope.enter_context(
                    nc.sbuf_tensor(f"abMG{i}", [128, TOK], BF16)) for i in range(2)]
                b.op("gpsimd", "iota", QROW[:], [[1, TOK]], channel_multiplier=0,
                     allow_small_or_imprecise_dtypes=True, waits=fence,
                     incs=[POOL.bump()])
                iota_done = POOL.val()
                mg_free = [None, None]
                for kt in range(32):
                    slm = kt % 2
                    b.op("vector", "tensor_scalar", MGEN[slm][:], QROW[:],
                         KADJ[:, kt:kt + 1], None, ALU.is_ge,
                         waits=[mg_free[slm], loads, iota_done] + fence,
                         incs=[DVE.bump()])
                    b.op("gpsimd", "dma_start", maskscr[kt], MGEN[slm][:],
                         waits=[DVE.val()], incs=[qA.bump(16)])
                    mg_free[slm] = qA.val()

            for t in range(NT):
                b.op("vector", "tensor_scalar", xr_own(t), xr_own(t), bcol(72 + t),
                     None, ALU.add, incs=[DVE.bump()])

            fence2 = [x.val() for x in (PE, DVE, ACT, POOL, qA, qW)]
            QHR = [psb(f"QHR{i}", [128, TOK], BF16) for i in range(2)]
            qhr_free = [None, None]
            qhr_ready = [None, None]
            X2T = [psb(f"X2T{i}", [128, TOK], BF16) for i in range(2)]
            x2_free = [None, None]
            KSQ = psb("KSQ", [128, TOK], BF16)
            KNB = psb("KNB", [128, TOK], F32)
            KHAT = psb("KHAT", [128, 2 * TOK], BF16)
            VTOK = psb("VTOK", [128, 32 * 130], BF16)
            BVF = psb("BVF", [1, H], F32)
            BVB = psb("BVB", [1, H], BF16)
            b.op("gpsimd", "dma_start", BVF[:], bv_in, waits=fence2,
                 incs=[qA.bump(16)])
            b.op("scalar", "activation", BVB[:], BVF[:], AF.Copy, waits=[qA.val()],
                 incs=[ACT.bump()])
            PT = psb("PT", [128, 2048], BF16)
            AT = psb("AT", [128, TOK], BF16)
            MSK = [psb(f"MSK{i}", [128, TOK], BF16) for i in range(2)]
            msk_free = [None, None]
            RB = psb("RB", [64, 512], BF16)
            RROW = psb("RROW", [1, 512], F32)
            SS2K = psb("SS2K", [2, TOK], F32)

            b.op("vector", "memset", VTOK[:], 1.0, waits=fence2,
                 incs=[DVE.bump()])
            vt_ones = DVE.val()

            def load_qh(pair):
                slq = pair % 2
                b.op("gpsimd", "dma_start", QHR[slq][:], qh_scr[pair],
                     waits=[qhat_out, qhr_free[slq]], incs=[qA.bump(16)])
                qhr_ready[slq] = qA.val()

            def load_x2(hh, half, slot, extra=()):
                b.op("gpsimd", "dma_start", X2T[slot][:],
                     cc_out.ap()[half * 1024 + hh * 128: half * 1024 + (hh + 1) * 128, :],
                     waits=[cc_done, x2_free[slot]] + list(extra), incs=[qA.bump(16)])
                return qA.val()

            load_qh(0)
            for pair in range(NT):
                if pair + 1 < NT:
                    load_qh(pair + 1)
                for half in range(2):
                    wsl = half % 2
                    stream_cols(WkT, pair, wsl)
                    lo = half * 2048
                    rowb = 2048 - lo
                    kp = PS8[:, lo:lo + 2048]
                    x2r = [None, None]
                    x2r[0] = load_x2(0, half, 0)
                    for hh in range(NT):
                        if hh + 1 < NT:
                            x2r[(hh + 1) % 2] = load_x2(hh + 1, half, (hh + 1) % 2)
                        wv = [x2r[hh % 2], wready[wsl]]
                        if hh == 0:
                            wv += bwaits(lo, lo + 2048)
                        for ch in range(4):
                            b.op("tensor", "matmul", kp[:, ch * 512:(ch + 1) * 512],
                                 WBB[wsl][:, ts128(hh)],
                                 X2T[hh % 2][:, ch * 512:(ch + 1) * 512],
                                 start=(hh == 0), stop=(hh == NT - 1),
                                 waits=wv if ch == 0 else [],
                                 incs=[PE.bump()] if (hh == NT - 1 and ch == 3) else [])
                        x2_free[hh % 2] = PE.val()
                    pk = PE.val()
                    wfree[wsl] = pk
                    b.op("scalar", "activation", KSQ[:], kp, AF.Square,
                         bias=bcol(56 + pair), waits=[pk], incs=[ACT.bump()])
                    ak = ACT.val()
                    sp = PS8[0:2, rowb:rowb + 512]
                    for ch in range(4):
                        b.op("tensor", "matmul", sp, SELABT[:],
                             KSQ[:, ch * 512:(ch + 1) * 512], start=True, stop=True,
                             waits=[ak] + bwaits(rowb, rowb + 512), incs=[PE.bump()])
                        b.op("vector", "tensor_copy",
                             SS2K[0:2, ch * 512:(ch + 1) * 512], sp,
                             waits=[PE.val()], incs=[DVE.bump()])
                        bfree(rowb, rowb + 512, DVE.val())
                    b.op("scalar", "activation", SS2K[0:2, :], SS2K[0:2, :], AF.Sqrt,
                         waits=[DVE.val()], incs=[ACT.bump()])
                    b.op("vector", "tensor_scalar", SS2K[0:2, :], SS2K[0:2, :], 1e-8,
                         None, ALU.max, waits=[ACT.val()], incs=[DVE.bump()])
                    b.op("vector", "reciprocal", SS2K[0:2, :], SS2K[0:2, :],
                         incs=[DVE.bump()])
                    rd = DVE.val()
                    for ch in range(4):
                        pb = PS8[:, rowb:rowb + 512]
                        b.op("tensor", "matmul", pb, SELAB2[:],
                             SS2K[0:2, ch * 512:(ch + 1) * 512], start=True, stop=True,
                             waits=[rd] + bwaits(rowb, rowb + 512), incs=[PE.bump()])
                        b.op("vector", "tensor_copy", KNB[:, ch * 512:(ch + 1) * 512],
                             pb, waits=[PE.val()], incs=[DVE.bump()])
                        bfree(rowb, rowb + 512, DVE.val())
                    nb = DVE.val()
                    b.op("vector", "scalar_tensor_tensor",
                         KHAT[:, half * TOK:(half + 1) * TOK], kp, bcol(56 + pair),
                         KNB[:], ALU.add, ALU.mult, waits=[nb], incs=[DVE.bump()])
                    bfree(lo, lo + 2048, DVE.val())
                khat_done = DVE.val()

                for half in range(2):
                    wsl = half % 2
                    stream_cols(WvT, pair, wsl)
                    lo = half * 2048
                    vp = PS8[:, lo:lo + 2048]
                    x2r = [None, None]
                    x2r[0] = load_x2(0, half, 0)
                    for hh in range(NT):
                        if hh + 1 < NT:
                            x2r[(hh + 1) % 2] = load_x2(hh + 1, half, (hh + 1) % 2)
                        wv = [x2r[hh % 2], wready[wsl]]
                        if hh == 0:
                            wv += bwaits(lo, lo + 2048)
                        for ktl in range(16):
                            b.op("tensor", "matmul", vp[:, ktl * 128:(ktl + 1) * 128],
                                 X2T[hh % 2][:, ktl * 128:(ktl + 1) * 128],
                                 WBB[wsl][:, ts128(hh)],
                                 start=(hh == 0), stop=False,
                                 waits=wv if ktl == 0 else [],
                                 incs=[PE.bump()] if ktl == 15 else [])
                        x2_free[hh % 2] = PE.val()
                    for ktl in range(16):
                        b.op("tensor", "matmul", vp[:, ktl * 128:(ktl + 1) * 128],
                             ONESRB[:], BVB[0:1, pair * 128:(pair + 1) * 128],
                             start=False, stop=True,
                             incs=[PE.bump()] if ktl == 15 else [])
                    pv = PE.val()
                    wfree[wsl] = pv
                    b.op("vector", "tensor_copy",
                         VTOK[:].rearrange("p (k c) -> p k c", k=32)[:, half * 16:(half + 1) * 16, 0:64],
                         vp.rearrange("p (k c) -> p k c", k=16)[:, :, 0:64],
                         waits=[pv, vt_ones], incs=[DVE.bump()])
                    b.op("vector", "tensor_copy",
                         VTOK[:].rearrange("p (k c) -> p k c", k=32)[:, half * 16:(half + 1) * 16, 65:129],
                         vp.rearrange("p (k c) -> p k c", k=16)[:, :, 64:128],
                         incs=[DVE.bump()])
                    bfree(lo, lo + 2048, DVE.val())
                vtok_done = DVE.val()

                scnt = [0]
                for head in range(2):
                    hsl = slice(head * 64, (head + 1) * 64)
                    groups = []
                    for kt in range(32):
                        qc0 = max(0, (kt - 16) // 4)
                        for g0 in range(qc0, 4, 2):
                            groups.append((kt, g0, [qc for qc in (g0, g0 + 1)
                                                    if qc < 4]))
                    pend = None
                    mready = None
                    last_mask_kt = [-1]

                    def emit_scores(kt, g0, qcs):
                        slm = kt % 2
                        if kt != last_mask_kt[0]:
                            b.op("gpsimd", "dma_start", MSK[slm][:], maskscr[kt],
                                 waits=[msk_free[slm]] + (fence2
                                                          if msk_free[slm] is None
                                                          else []),
                                 incs=[qA.bump(16)])
                            last_mask_kt[0] = kt
                        mr = qA.val()
                        ng = len(qcs)
                        slot = scnt[0] % 2
                        scnt[0] += 1
                        ssl = slot * 1024
                        scp = PS8[:, ssl:ssl + 1024]
                        for j, qc in enumerate(qcs):
                            b.op("tensor", "matmul", scp[:, j * 512:(j + 1) * 512],
                                 KHAT[hsl, kt * 128:(kt + 1) * 128],
                                 QHR[pair % 2][hsl, qc * 512:(qc + 1) * 512],
                                 start=True, stop=True,
                                 waits=([vtok_done, qhr_ready[pair % 2]]
                                        + bwaits(ssl, ssl + 1024)) if j == 0 else [],
                                 incs=[PE.bump()] if j == ng - 1 else [])
                        psc = PE.val()
                        b.op("scalar", "activation", PT[:, slot * 1024:slot * 1024 + ng * 512],
                             scp[:, 0:ng * 512], AF.Exp, scale=CONSTS[:, 1:2],
                             waits=[psc], incs=[ACT.bump()])
                        bfree(ssl, ssl + 1024, ACT.val())
                        b.op("vector", "tensor_tensor",
                             PT[:, slot * 1024:slot * 1024 + ng * 512],
                             PT[:, slot * 1024:slot * 1024 + ng * 512],
                             MSK[kt % 2][:, g0 * 512:(g0 + ng) * 512],
                             ALU.mult, waits=[ACT.val(), mr], incs=[DVE.bump()])
                        pm = DVE.val()
                        msk_free[kt % 2] = pm
                        return (kt, qcs, slot, pm)

                    def emit_av(st):
                        kt, qcs, slot, pm = st
                        for j, qc in enumerate(qcs):
                            last_kt = min(31, 19 + 4 * qc)
                            ov = PS8[0:65, 2048 + qc * 512: 2048 + (qc + 1) * 512]
                            b.op("tensor", "matmul", ov,
                                 VTOK[:, kt * 130 + head * 65:
                                      kt * 130 + (head + 1) * 65],
                                 PT[:, slot * 1024 + j * 512:
                                    slot * 1024 + (j + 1) * 512],
                                 start=(kt == 0), stop=(kt == last_kt),
                                 waits=[pm] + (bwaits(2048 + qc * 512,
                                                      2048 + (qc + 1) * 512)
                                               if kt == 0 else []),
                                 incs=[PE.bump()] if (kt == last_kt) else [])

                    for g in groups:
                        st = emit_scores(*g)
                        if pend is not None:
                            emit_av(pend)
                        pend = st
                    emit_av(pend)
                    pav = PE.val()
                    for qc in range(4):
                        ov = PS8[0:65, 2048 + qc * 512: 2048 + (qc + 1) * 512]
                        b.op("vector", "reciprocal", RROW[:], ov[64:65, :],
                             waits=[pav], incs=[DVE.bump()])
                        rv = DVE.val()
                        bb2 = PS8[0:64, 0:512]
                        b.op("tensor", "matmul", bb2, ONESRF[0:1, 0:64], RROW[:],
                             start=True, stop=True,
                             waits=[rv, consts_dve] + bwaits(0, 512), incs=[PE.bump()])
                        b.op("scalar", "activation", RB[:], bb2, AF.Copy,
                             waits=[PE.val()], incs=[ACT.bump()])
                        bfree(0, 512, ACT.val())
                        b.op("vector", "tensor_tensor",
                             AT[hsl, qc * 512:(qc + 1) * 512], ov[0:64, :], RB[:],
                             ALU.mult, waits=[ACT.val()], incs=[DVE.bump()])
                        bfree(2048 + qc * 512, 2048 + (qc + 1) * 512, DVE.val())
                at_done = DVE.val()
                qhr_free[pair % 2] = at_done

                wsl = 2
                stream_rows(WoT, pair, wsl)
                for ct in range(NT):
                    lo = (ct % 2) * 2048
                    wop = PS8[:, lo:lo + 2048]
                    wv0 = [wready[wsl], at_done] + bwaits(lo, lo + 2048)
                    for ch in range(4):
                        b.op("tensor", "matmul", wop[:, ch * 512:(ch + 1) * 512],
                             WBB[wsl][:, ts128(ct)], AT[:, ch * 512:(ch + 1) * 512],
                             start=True, stop=True, waits=wv0 if ch == 0 else [],
                             incs=[PE.bump()] if ch == 3 else [])
                    pw = PE.val()
                    b.op("vector", "scalar_tensor_tensor", xr_own(ct), wop, 0.5,
                         xr_own(ct), ALU.mult, ALU.add, waits=[pw], incs=[DVE.bump()])
                    bfree(lo, lo + 2048, DVE.val())
                wfree[wsl] = PE.val()

    if stage == "full":
        emit_ffn("f2")

    allv = [s.val() for s in (PE, DVE, ACT, POOL, qA, qW)]
    if stage != "dbgffn":
        b.op("sync", "dma_start", out_T.rearrange("(t p) n -> p t n", p=128),
             XR[:].rearrange("p (t n) -> p t n", t=NT)[:, :, 256:EXT],
             waits=allv, incs=[qW.bump(16)])
    else:
        b.op("sync", "dma_start", rowscr[6, 0:16], ROWS[0:1, 0:16],
             waits=allv, incs=[qW.bump(16)])
    fin = qW.val()

    def final_wait(eng, fin=fin):
        eng.wait_ge(fin[0].h, fin[1])
    b.prog["sync"].append(final_wait)

    with nc.Block() as block:
        @block.tensor
        def _(eng):
            for f in b.prog["tensor"]:
                f(eng)

        @block.vector
        def _(eng):
            for f in b.prog["vector"]:
                f(eng)

        @block.scalar
        def _(eng):
            for f in b.prog["scalar"]:
                f(eng)

        @block.gpsimd
        def _(eng):
            for f in b.prog["gpsimd"]:
                f(eng)

        @block.sync
        def _(eng):
            for f in b.prog["sync"]:
                f(eng)

    top.close()
    return nc



def _per_partition(vec, ntiles):
    return np.ascontiguousarray(np.asarray(vec, np.float32).reshape(ntiles, 128).T)


def kernel(**inputs):
    x = np.asarray(inputs["L0_attn_a_x"], np.float32)

    biaspk = np.zeros((128, 128), np.float32)
    biaspk[:, 0:8] = _per_partition(inputs["L0_attn_a_b_ret"], 8)
    biaspk[:, 8:16] = _per_partition(inputs["L0_attn_a_ln1_scale"], 8)
    biaspk[:, 16:24] = _per_partition(inputs["L0_attn_a_ln1_bias"], 8)
    biaspk[:, 24:32] = _per_partition(inputs["L0_attn_a_b2"], 8)
    biaspk[:, 32:40] = _per_partition(inputs["L0_attn_a_ln2_scale"], 8)
    biaspk[:, 40:48] = _per_partition(inputs["L0_attn_a_ln2_bias"], 8)
    biaspk[:, 48:56] = _per_partition(inputs["L0_attn_b_bq"], 8)
    biaspk[:, 56:64] = _per_partition(inputs["L0_attn_b_bk"], 8)
    biaspk[:, 64:72] = _per_partition(inputs["L0_attn_b_bo"], 8)

    common = dict(
        Wret=np.ascontiguousarray(inputs["L0_attn_a_W_ret"], np.float32),
        W1a=np.ascontiguousarray(inputs["L0_attn_a_W1"], np.float32),
        W2a=np.ascontiguousarray(inputs["L0_attn_a_W2"], np.float32),
        w1s=np.ascontiguousarray(inputs["L0_ffn_w1s"], np.float32),
        w2s=np.ascontiguousarray(inputs["L0_ffn_w2s"], np.float32),
        w3s=np.ascontiguousarray(inputs["L0_ffn_w3s"], np.float32),
        W1e=np.ascontiguousarray(inputs["L0_ffn_W1"], np.float32),
        W2e=np.ascontiguousarray(inputs["L0_ffn_W2"], np.float32),
        W3e=np.ascontiguousarray(inputs["L0_ffn_W3"], np.float32),
        Wg=np.ascontiguousarray(np.pad(
            np.asarray(inputs["L0_ffn_Wg"], np.float32), ((0, 0), (0, 120)))),
        WqT=np.ascontiguousarray(np.asarray(inputs["L0_attn_b_Wq"], np.float32).T),
        WkT=np.ascontiguousarray(np.asarray(inputs["L0_attn_b_Wk"], np.float32).T),
        WvT=np.ascontiguousarray(np.asarray(inputs["L0_attn_b_Wv"], np.float32).T),
        WoT=np.ascontiguousarray(np.asarray(inputs["L0_attn_b_Wo"], np.float32).T),
        biaspk=biaspk,
        b1a_in=_per_partition(inputs["L0_attn_a_b1"], 32),
        bv_in=np.ascontiguousarray(np.asarray(inputs["L0_attn_b_bv"], np.float32)[None, :]),
        identf_in=np.eye(128, dtype=np.float32),
        selabT_in=np.ascontiguousarray(
            np.stack([(np.arange(128) < 64), (np.arange(128) >= 64)], 1).astype(BF)),
        selab2_in=np.ascontiguousarray(
            np.stack([(np.arange(128) < 64), (np.arange(128) >= 64)], 0)
            .astype(np.float32)),
    )

    in_maps = []
    for c in range(8):
        bb, hh = c // 2, c % 2
        start = hh * TOK
        xe = np.zeros((EXT, H), np.float32)
        lo = start - 256
        if lo < 0:
            xe[256:] = x[bb, 0:start + TOK]
        else:
            xe[:] = x[bb, lo:start + TOK]
        kadj = (np.arange(32)[None, :] * 128 + np.arange(128)[:, None]
                - start).astype(np.float32)
        extm = np.full((128, 256), 1.0 if hh == 1 else 0.0, np.float32)
        m = dict(common)
        m["xT"] = np.ascontiguousarray(xe.T)
        m["kadj_in"] = kadj
        m["extmask_in"] = extm
        in_maps.append(m)

    nc = build_program(_STAGE)
    res = run_bass_kernel_spmd(nc, in_maps, core_ids=list(range(8)))

    out = np.empty((4, 4096, H), np.float32)
    for c in range(8):
        bb, hh = c // 2, c % 2
        out[bb, hh * TOK:(hh + 1) * TOK, :] = res.results[c]["out_T"].T
    return out
